# version 03
# baseline (speedup 1.0000x reference)
import contextlib
import numpy as np
import ml_dtypes
import concourse.bass as bass
import concourse.mybir as mybir
from concourse.bass_utils import run_bass_kernel_spmd

F32 = mybir.dt.float32
BF16 = mybir.dt.bfloat16
AF = mybir.ActivationFunctionType
ALU = mybir.AluOpType
AX = mybir.AxisListType
BF = ml_dtypes.bfloat16

T, D = 2048, 1024
NT, ND = 16, 8
H, DH = 16, 64
E = 8
F0 = 4096
LNEPS = 1e-5
A0EPS = 1e-6
G1EPS = 1e-4
WTHR = 0.1
GC0 = 0.044715
GC1 = 0.7978845608028654


class SP:
    def __init__(self):
        self.steps = []

    def dma(self, q, out, in_, **kw):
        self.steps.append((q, "dma_start", dict(out=out, in_=in_, **kw), 16))

    def on(self, ek, meth, **kw):
        self.steps.append((ek, meth, kw, 1))


def emit(nc, sp):
    total = sum(s[3] for s in sp.steps)
    with nc.semaphore("g") as g, nc.Block("main") as blk:
        def make(ek, final=False):
            def run(eng):
                c = 0
                for (k, meth, kw, inc) in sp.steps:
                    if k == ek:
                        if c > 0:
                            eng.wait_ge(g, c)
                        getattr(eng, meth)(**kw).then_inc(g, inc)
                    c += inc
                if final:
                    eng.wait_ge(g, total)
            return run
        blk.sync(make("sync", final=True))
        blk.gpsimd(make("gpsimd"))
        blk.vector(make("vector"))
        blk.scalar(make("scalar"))
        blk.tensor(make("tensor"))


def _bass():
    return bass.Bass("TRN2", target_bir_lowering=False)


def build_ln_gate(residual):
    nc = _bass()
    dt = nc.dram_tensor
    x_h = dt("x", [T, D], F32, kind="ExternalInput").ap()
    if residual:
        a_h = dt("a0", [T, D], F32, kind="ExternalInput").ap()
        f_h = dt("f0", [T, D], F32, kind="ExternalInput").ap()
        x1_h = dt("x1", [T, D], F32, kind="ExternalOutput").ap()
    gw_h = dt("gw", [D, E], F32, kind="ExternalInput").ap()
    gb_h = dt("gb", [1, E], F32, kind="ExternalInput").ap()
    idb_h = dt("idb", [128, 128], BF16, kind="ExternalInput").ap()
    xnf_h = dt("xnf", [ND, 128, T], BF16, kind="ExternalOutput").ap()
    gl_h = dt("glog", [T, E], F32, kind="ExternalOutput").ap()

    sp = SP()
    es = contextlib.ExitStack()
    with es:
        X32 = es.enter_context(nc.sbuf_tensor("X32", [128, D], F32))
        A32 = es.enter_context(nc.sbuf_tensor("A32", [128, D], F32))
        B32 = es.enter_context(nc.sbuf_tensor("B32", [128, D], F32))
        SQ = es.enter_context(nc.sbuf_tensor("SQ", [128, D], F32))
        st = es.enter_context(nc.sbuf_tensor("st", [128, 12], F32))
        xh = es.enter_context(nc.sbuf_tensor("xh", [128, D], BF16))
        xl = es.enter_context(nc.sbuf_tensor("xl", [128, D], BF16))
        XNH = es.enter_context(nc.sbuf_tensor("XNH", [128, D], BF16))
        XNL = es.enter_context(nc.sbuf_tensor("XNL", [128, D], BF16))
        sw32 = es.enter_context(nc.sbuf_tensor("sw32", [128, ND, E], F32))
        swh = es.enter_context(nc.sbuf_tensor("swh", [128, ND, E], BF16))
        swl = es.enter_context(nc.sbuf_tensor("swl", [128, ND, E], BF16))
        gbrow = es.enter_context(nc.sbuf_tensor("gbrow", [1, E], F32))
        ones1 = es.enter_context(nc.sbuf_tensor("ones1", [1, 128], F32))
        idb = es.enter_context(nc.sbuf_tensor("idb_s", [128, 128], BF16))
        gl32 = es.enter_context(nc.sbuf_tensor("gl32", [128, E], F32))
        psT = es.enter_context(nc.psum_tensor("psT", [128, 2048], BF16))
        psG = es.enter_context(nc.psum_tensor("psG", [128, E], F32))
        gw_v = gw_h.rearrange("(c p) e -> p c e", p=128)
        sp.on("vector", "memset", ap=ones1[:, :], constant=1.0)
        sp.dma("sync", out=idb[:, :], in_=idb_h)
        sp.dma("sync", out=gbrow[:, :], in_=gb_h)
        for dc in range(ND):
            sp.dma("sync", out=sw32[:, dc, :], in_=gw_v[:, dc, :])
        sp.on("vector", "tensor_copy", out=swh[:, :, :], in_=sw32[:, :, :])
        sp.on("vector", "tensor_tensor", out=swl[:, :, :], in0=sw32[:, :, :],
              in1=swh[:, :, :], op=ALU.subtract)

        for tt in range(NT):
            r = slice(tt * 128, (tt + 1) * 128)
            sp.dma("sync", out=X32[:, :], in_=x_h[r, :])
            if residual:
                sp.dma("sync", out=A32[:, :], in_=a_h[r, :])
                sp.dma("sync", out=B32[:, :], in_=f_h[r, :])
                sp.on("vector", "tensor_tensor", out=X32[:, :], in0=X32[:, :],
                      in1=A32[:, :], op=ALU.add)
                sp.on("vector", "tensor_tensor", out=X32[:, :], in0=X32[:, :],
                      in1=B32[:, :], op=ALU.add)
                sp.dma("gpsimd", out=x1_h[r, :], in_=X32[:, :])
            sp.on("vector", "reduce_sum", out=st[:, 0:1], in_=X32[:, :], axis=AX.X)
            sp.on("vector", "tensor_scalar", out=st[:, 1:2], in0=st[:, 0:1],
                  scalar1=1.0 / D, scalar2=None, op0=ALU.mult)
            sp.on("scalar", "activation", out=SQ[:, :], in_=X32[:, :],
                  func=AF.Square, accum_out=st[:, 2:3])
            sp.on("vector", "tensor_scalar", out=st[:, 3:4], in0=st[:, 2:3],
                  scalar1=1.0 / D, scalar2=None, op0=ALU.mult)
            sp.on("vector", "scalar_tensor_tensor", out=st[:, 4:5], in0=st[:, 1:2],
                  scalar=st[:, 1:2], in1=st[:, 3:4], op0=ALU.mult, op1=ALU.subtract)
            sp.on("vector", "tensor_scalar", out=st[:, 5:6], in0=st[:, 4:5],
                  scalar1=-1.0, scalar2=LNEPS, op0=ALU.mult, op1=ALU.add)
            sp.on("scalar", "activation", out=st[:, 6:7], in_=st[:, 5:6], func=AF.Sqrt)
            sp.on("vector", "reciprocal", out=st[:, 7:8], in_=st[:, 6:7])
            sp.on("vector", "tensor_tensor", out=st[:, 8:9], in0=st[:, 1:2],
                  in1=st[:, 7:8], op=ALU.mult)
            sp.on("vector", "tensor_scalar", out=SQ[:, :], in0=X32[:, :],
                  scalar1=st[:, 7:8], scalar2=st[:, 8:9], op0=ALU.mult,
                  op1=ALU.subtract)
            sp.on("vector", "tensor_copy", out=xh[:, :], in_=SQ[:, :])
            sp.on("vector", "tensor_tensor", out=xl[:, :], in0=SQ[:, :],
                  in1=xh[:, :], op=ALU.subtract)
            for dc in range(ND):
                c = slice(dc * 128, (dc + 1) * 128)
                sp.on("tensor", "transpose", out=psT[:, c], in_=xh[:, c],
                      identity=idb[:, :])
            for dc in range(ND):
                sp.on("tensor", "transpose", out=psT[:, 1024 + dc * 128:1024 + (dc + 1) * 128],
                      in_=xl[:, dc * 128:(dc + 1) * 128], identity=idb[:, :])
            sp.on("vector", "tensor_copy", out=XNH[:, :], in_=psT[:, 0:1024])
            sp.on("vector", "tensor_copy", out=XNL[:, :], in_=psT[:, 1024:2048])
            XNHv = XNH[:, :].rearrange("p (c t) -> p c t", t=128)
            XNLv = XNL[:, :].rearrange("p (c t) -> p c t", t=128)
            sp.on("tensor", "matmul", out=psG[:, :], lhsT=ones1[:, :],
                  rhs=gbrow[:, :], start=True, stop=False)
            for dc in range(ND):
                last = dc == ND - 1
                sp.on("tensor", "matmul", out=psG[:, :], lhsT=XNHv[:, dc, :],
                      rhs=swh[:, dc, :], start=False, stop=False)
                sp.on("tensor", "matmul", out=psG[:, :], lhsT=XNHv[:, dc, :],
                      rhs=swl[:, dc, :], start=False, stop=False)
                sp.on("tensor", "matmul", out=psG[:, :], lhsT=XNLv[:, dc, :],
                      rhs=swh[:, dc, :], start=False, stop=last)
            sp.on("vector", "tensor_copy", out=gl32[:, :], in_=psG[:, :])
            sp.dma("gpsimd", out=gl_h[r, :], in_=gl32[:, :])
            for dc in range(ND):
                sp.dma("gpsimd", out=xnf_h[dc, :, r],
                       in_=XNH[:, dc * 128:(dc + 1) * 128])
        emit(nc, sp)
    return nc


def build_attn0():
    nc = _bass()
    dt = nc.dram_tensor
    xnf_h = dt("xnf", [ND, 128, T], BF16, kind="ExternalInput").ap()
    wq_h = dt("wqT", [D, D], F32, kind="ExternalInput").ap()
    wk_h = dt("wkT", [D, D], F32, kind="ExternalInput").ap()
    wv_h = dt("wvT", [D, D], F32, kind="ExternalInput").ap()
    wo_h = dt("woT", [D, D], F32, kind="ExternalInput").ap()
    bq_h = dt("bq", [1, D], F32, kind="ExternalInput").ap()
    bk_h = dt("bk", [1, D], F32, kind="ExternalInput").ap()
    bv_h = dt("bv", [1, D], F32, kind="ExternalInput").ap()
    bo_h = dt("bo", [1, D], F32, kind="ExternalInput").ap()
    idb_h = dt("idb", [128, 128], BF16, kind="ExternalInput").ap()
    a0_h = dt("a0", [T, D], F32, kind="ExternalOutput").ap()

    sp = SP()
    es = contextlib.ExitStack()
    with es:
        xns = es.enter_context(nc.sbuf_tensor("xns", [128, ND, T], BF16))
        wq_s = es.enter_context(nc.sbuf_tensor("wq_s", [128, ND, D], BF16))
        wk_s = es.enter_context(nc.sbuf_tensor("wk_s", [128, ND, D], BF16))
        wv_s = es.enter_context(nc.sbuf_tensor("wv_s", [128, ND, D], BF16))
        wo_s = es.enter_context(nc.sbuf_tensor("wo_s", [128, ND, D], BF16))
        brow = es.enter_context(nc.sbuf_tensor("brow", [1, 4 * D], F32))
        ones1 = es.enter_context(nc.sbuf_tensor("ones1", [1, 128], F32))
        onec = es.enter_context(nc.sbuf_tensor("onec", [1, 1], F32))
        idb = es.enter_context(nc.sbuf_tensor("idb_s", [128, 128], BF16))
        BK32 = es.enter_context(nc.sbuf_tensor("BK32", [128, D], F32))
        BV32 = es.enter_context(nc.sbuf_tensor("BV32", [128, D], F32))
        BO32 = es.enter_context(nc.sbuf_tensor("BO32", [128, D], F32))
        bqF = es.enter_context(nc.sbuf_tensor("bqF", [128, ND], F32))
        SA = es.enter_context(nc.sbuf_tensor("SA", [128, D], F32))
        SB = es.enter_context(nc.sbuf_tensor("SB", [128, D], F32))
        SC = es.enter_context(nc.sbuf_tensor("SC", [128, D], F32))
        Ktm = es.enter_context(nc.sbuf_tensor("Ktm", [128, D], BF16))
        Vext = es.enter_context(nc.sbuf_tensor("Vext", [128, H, 66], BF16))
        KVb = es.enter_context(nc.sbuf_tensor("KVb", [128, 520], BF16))
        Qf = es.enter_context(nc.sbuf_tensor("Qf", [128, D], BF16))
        nrm = es.enter_context(nc.sbuf_tensor("nrm", [128, 8], F32))
        rn = es.enter_context(nc.sbuf_tensor("rn", [128, 8], F32))
        AO32 = es.enter_context(nc.sbuf_tensor("AO32", [128, D], F32))
        AObf = es.enter_context(nc.sbuf_tensor("AObf", [128, D], BF16))
        Af = es.enter_context(nc.sbuf_tensor("Af", [128, D], BF16))
        OUT32 = es.enter_context(nc.sbuf_tensor("OUT32", [128, D], F32))
        psA = es.enter_context(nc.psum_tensor("psA", [128, D], F32))
        psV = es.enter_context(nc.psum_tensor("psV", [128, D], F32))
        psKV = es.enter_context(nc.psum_tensor("psKV", [128, 520], F32))
        psTa = es.enter_context(nc.psum_tensor("psTa", [128, D], BF16))
        sp.on("vector", "memset", ap=ones1[:, :], constant=1.0)
        sp.on("vector", "memset", ap=onec[:, :], constant=1.0)
        sp.on("vector", "memset",
              ap=Vext[:, :, :].rearrange("p a b -> p (a b)"), constant=1.0)
        sp.dma("sync", out=idb[:, :], in_=idb_h)
        for name, src in (("q", bq_h), ("k", bk_h), ("v", bv_h), ("o", bo_h)):
            off = {"q": 0, "k": D, "v": 2 * D, "o": 3 * D}[name]
            sp.dma("sync", out=brow[:, off:off + D], in_=src)
        for dc in range(ND):
            c = slice(dc * 128, (dc + 1) * 128)
            sp.dma("sync", out=xns[:, dc, :], in_=xnf_h[dc, :, :])
            sp.dma("gpsimd", out=wq_s[:, dc, :], in_=wq_h[c, :])
            sp.dma("gpsimd", out=wk_s[:, dc, :], in_=wk_h[c, :])
            sp.dma("gpsimd", out=wv_s[:, dc, :], in_=wv_h[c, :])
            sp.dma("gpsimd", out=wo_s[:, dc, :], in_=wo_h[c, :])
        for dst, off in ((BK32, D), (BV32, 2 * D), (BO32, 3 * D)):
            for half in range(2):
                hc = slice(half * 512, (half + 1) * 512)
                sp.on("tensor", "matmul", out=psA[:, hc], lhsT=ones1[:, :],
                      rhs=brow[:, off + half * 512:off + (half + 1) * 512],
                      start=True, stop=True)
            sp.on("vector", "tensor_copy", out=dst[:, :], in_=psA[:, :])
        for dc in range(ND):
            sp.on("tensor", "matmul", out=psKV[:, dc:dc + 1],
                  lhsT=brow[:, dc * 128:(dc + 1) * 128], rhs=onec[:, :],
                  start=True, stop=True)
        sp.on("vector", "tensor_copy", out=bqF[:, :], in_=psKV[:, 0:ND])

        for tt in range(NT):
            for half in range(2):
                hc = slice(half * 512, (half + 1) * 512)
                for dc in range(ND):
                    sp.on("tensor", "matmul", out=psA[:, hc],
                          lhsT=xns[:, dc, tt * 128:(tt + 1) * 128],
                          rhs=wk_s[:, dc, hc],
                          start=(dc == 0), stop=(dc == ND - 1))
                for dc in range(ND):
                    sp.on("tensor", "matmul", out=psV[:, hc],
                          lhsT=xns[:, dc, tt * 128:(tt + 1) * 128],
                          rhs=wv_s[:, dc, hc],
                          start=(dc == 0), stop=(dc == ND - 1))
            sp.on("vector", "tensor_tensor", out=SA[:, :], in0=psA[:, :],
                  in1=BK32[:, :], op=ALU.add)
            sp.on("vector", "tensor_scalar", out=SB[:, :], in0=SA[:, :],
                  scalar1=0.0, scalar2=None, op0=ALU.max)
            sp.on("vector", "tensor_scalar", out=SC[:, :], in0=SA[:, :],
                  scalar1=0.0, scalar2=None, op0=ALU.min)
            sp.on("scalar", "activation", out=SC[:, :], in_=SC[:, :], func=AF.Exp)
            sp.on("vector", "tensor_tensor", out=Ktm[:, :], in0=SB[:, :],
                  in1=SC[:, :], op=ALU.add)
            sp.on("vector", "tensor_tensor",
                  out=Vext[:, :, 0:64],
                  in0=psV[:, :].rearrange("p (h d) -> p h d", d=64),
                  in1=BV32[:, :].rearrange("p (h d) -> p h d", d=64), op=ALU.add)
            for h in range(H):
                po = (h % 2) * 64
                sp.on("tensor", "matmul",
                      out=psKV[po:po + 64, (h // 2) * 65:(h // 2) * 65 + 65],
                      lhsT=Ktm[:, h * 64:(h + 1) * 64], rhs=Vext[:, h, 0:65],
                      start=(tt == 0), stop=(tt == NT - 1), skip_group_check=True)
        sp.on("vector", "tensor_copy", out=KVb[:, :], in_=psKV[:, :])

        psO = psKV
        psOv = psO[:, 0:520].rearrange("p (h c) -> p h c", c=65)
        for tt in range(NT):
            tr = slice(tt * 128, (tt + 1) * 128)
            for pr in range(ND):
                for dc in range(ND):
                    sp.on("tensor", "matmul", out=psA[:, pr * 128:(pr + 1) * 128],
                          lhsT=wq_s[:, dc, pr * 128:(pr + 1) * 128],
                          rhs=xns[:, dc, tr],
                          start=(dc == 0), stop=(dc == ND - 1))
            for pr in range(ND):
                c = slice(pr * 128, (pr + 1) * 128)
                sp.on("vector", "tensor_scalar", out=SB[:, c], in0=psA[:, c],
                      scalar1=bqF[:, pr:pr + 1], scalar2=0.0, op0=ALU.add,
                      op1=ALU.max)
                sp.on("vector", "tensor_scalar", out=SC[:, c], in0=psA[:, c],
                      scalar1=bqF[:, pr:pr + 1], scalar2=0.0, op0=ALU.add,
                      op1=ALU.min)
            sp.on("scalar", "activation", out=SC[:, :], in_=SC[:, :], func=AF.Exp)
            sp.on("vector", "tensor_tensor", out=Qf[:, :], in0=SB[:, :],
                  in1=SC[:, :], op=ALU.add)
            for g8 in range(2):
                for hh in range(8):
                    h = g8 * 8 + hh
                    po = (h % 2) * 64
                    sp.on("tensor", "matmul", out=psO[:, hh * 65:hh * 65 + 65],
                          lhsT=Qf[po:po + 64, (h // 2) * 128:(h // 2) * 128 + 128],
                          rhs=KVb[po:po + 64, (h // 2) * 65:(h // 2) * 65 + 65],
                          start=True, stop=True, skip_group_check=True)
                sp.on("vector", "tensor_scalar", out=nrm[:, :],
                      in0=psOv[:, :, 64:65], scalar1=A0EPS, scalar2=None,
                      op0=ALU.add)
                sp.on("vector", "reciprocal", out=rn[:, :], in_=nrm[:, :])
                for hh in range(8):
                    h = g8 * 8 + hh
                    sp.on("vector", "tensor_scalar",
                          out=AO32[:, h * 64:(h + 1) * 64],
                          in0=psOv[:, hh, 0:64], scalar1=rn[:, hh:hh + 1],
                          scalar2=None, op0=ALU.mult)
            sp.on("vector", "tensor_copy", out=AObf[:, :], in_=AO32[:, :])
            for dc in range(ND):
                c = slice(dc * 128, (dc + 1) * 128)
                sp.on("tensor", "transpose", out=psTa[:, c], in_=AObf[:, c],
                      identity=idb[:, :])
            sp.on("vector", "tensor_copy", out=Af[:, :], in_=psTa[:, :])
            for half in range(2):
                hc = slice(half * 512, (half + 1) * 512)
                for dc in range(ND):
                    sp.on("tensor", "matmul", out=psV[:, hc],
                          lhsT=Af[:, dc * 128:(dc + 1) * 128],
                          rhs=wo_s[:, dc, hc],
                          start=(dc == 0), stop=(dc == ND - 1))
            sp.on("vector", "tensor_tensor", out=OUT32[:, :], in0=psV[:, :],
                  in1=BO32[:, :], op=ALU.add)
            sp.dma("gpsimd", out=a0_h[tr, :], in_=OUT32[:, :])
        emit(nc, sp)
    return nc


def build_ffn0():
    nc = _bass()
    dt = nc.dram_tensor
    xnf_h = dt("xnf", [ND, 128, T], BF16, kind="ExternalInput").ap()
    gl_h = dt("glog", [T, E], F32, kind="ExternalInput").ap()
    w1_h = dt("w1", [E, D, F0], F32, kind="ExternalInput").ap()
    b1_h = dt("b1", [E, F0], F32, kind="ExternalInput").ap()
    w2_h = dt("w2", [E, F0, D], F32, kind="ExternalInput").ap()
    b2_h = dt("b2", [E, D], F32, kind="ExternalInput").ap()
    f0_h = dt("f0", [T, D], F32, kind="ExternalOutput").ap()

    NFC = F0 // 128
    sp = SP()
    es = contextlib.ExitStack()
    with es:
        w1_s = es.enter_context(nc.sbuf_tensor("w1_s", [128, ND, F0], BF16))
        w2_s = es.enter_context(nc.sbuf_tensor("w2_s", [128, NFC, D], BF16))
        xnt = es.enter_context(nc.sbuf_tensor("xnt", [128, ND, 512], BF16))
        hbf = es.enter_context(nc.sbuf_tensor("hbf", [128, NFC, 512], BF16))
        b1row = es.enter_context(nc.sbuf_tensor("b1row", [1, D], F32))
        b2row = es.enter_context(nc.sbuf_tensor("b2row", [1, D], F32))
        b1col = es.enter_context(nc.sbuf_tensor("b1col", [128, NFC], F32))
        B2B = es.enter_context(nc.sbuf_tensor("B2B", [128, D], F32))
        ones1 = es.enter_context(nc.sbuf_tensor("ones1", [1, 128], F32))
        onec = es.enter_context(nc.sbuf_tensor("onec", [1, 1], F32))
        l8 = es.enter_context(nc.sbuf_tensor("l8", [128, E], F32))
        SEL = es.enter_context(nc.sbuf_tensor("SEL", [128, NT, E], F32))
        RPM = es.enter_context(nc.sbuf_tensor("RPM", [128, NT], F32))
        st = es.enter_context(nc.sbuf_tensor("st", [128, 4], F32))
        GA = es.enter_context(nc.sbuf_tensor("GA", [128, 512], F32))
        GB = es.enter_context(nc.sbuf_tensor("GB", [128, 512], F32))
        Y32 = es.enter_context(nc.sbuf_tensor("Y32", [128, D], F32))
        OUT32 = es.enter_context(nc.sbuf_tensor("OUT32", [128, D], F32))
        psH = es.enter_context(nc.psum_tensor("psH", [128, 512], F32))
        psY = es.enter_context(nc.psum_tensor("psY", [128, D], F32))
        psC = es.enter_context(nc.psum_tensor("psC", [128, NFC], F32))
        sp.on("vector", "memset", ap=ones1[:, :], constant=1.0)
        sp.on("vector", "memset", ap=onec[:, :], constant=1.0)
        for tt in range(NT):
            r = slice(tt * 128, (tt + 1) * 128)
            sp.dma("sync", out=l8[:, :], in_=gl_h[r, :])
            sp.on("vector", "reduce_max", out=st[:, 0:1], in_=l8[:, :],
                  axis=AX.X, negate=True)
            sp.on("scalar", "activation", out=SEL[:, tt, :], in_=l8[:, :],
                  func=AF.Exp, bias=st[:, 0:1], accum_out=st[:, 1:2])
            sp.on("vector", "reciprocal", out=RPM[:, tt:tt + 1], in_=st[:, 1:2])
            sp.on("vector", "tensor_scalar", out=SEL[:, tt, :], in0=l8[:, :],
                  scalar1=st[:, 0:1], scalar2=0.0, op0=ALU.add, op1=ALU.is_equal)
        for e in range(E):
            w1v = w1_h[e].rearrange("(c p) f -> p c f", p=128)
            w2v = w2_h[e].rearrange("(c p) f -> p c f", p=128)
            for dc in range(ND):
                sp.dma("gpsimd", out=w1_s[:, dc, :], in_=w1v[:, dc, :])
            for fc in range(NFC):
                sp.dma("gpsimd", out=w2_s[:, fc, :], in_=w2v[:, fc, :])
            sp.dma("sync", out=b2row[:, :], in_=b2_h[e:e + 1, :])
            for ch in range(4):
                sp.dma("sync", out=b1row[:, :],
                       in_=b1_h[e:e + 1, ch * D:(ch + 1) * D])
                for j in range(8):
                    sp.on("tensor", "matmul", out=psC[:, ch * 8 + j:ch * 8 + j + 1],
                          lhsT=b1row[:, j * 128:(j + 1) * 128], rhs=onec[:, :],
                          start=True, stop=True)
            sp.on("vector", "tensor_copy", out=b1col[:, :], in_=psC[:, :])
            for half in range(2):
                sp.on("tensor", "matmul", out=psY[:, half * 512:(half + 1) * 512],
                      lhsT=ones1[:, :], rhs=b2row[:, half * 512:(half + 1) * 512],
                      start=True, stop=True)
            sp.on("vector", "tensor_copy", out=B2B[:, :], in_=psY[:, :])
            for sl in range(4):
                sr = slice(sl * 512, (sl + 1) * 512)
                for dc in range(ND):
                    sp.dma("sync", out=xnt[:, dc, :], in_=xnf_h[dc, :, sr])
                for fc in range(NFC):
                    for dc in range(ND):
                        sp.on("tensor", "matmul", out=psH[:, :],
                              lhsT=w1_s[:, dc, fc * 128:(fc + 1) * 128],
                              rhs=xnt[:, dc, :],
                              start=(dc == 0), stop=(dc == ND - 1))
                    sp.on("vector", "tensor_scalar", out=GA[:, :], in0=psH[:, :],
                          scalar1=b1col[:, fc:fc + 1], scalar2=None, op0=ALU.add)
                    sp.on("scalar", "activation", out=GB[:, :], in_=GA[:, :],
                          func=AF.Square)
                    sp.on("vector", "tensor_scalar", out=GB[:, :], in0=GB[:, :],
                          scalar1=GC0, scalar2=1.0, op0=ALU.mult, op1=ALU.add)
                    sp.on("vector", "tensor_tensor", out=GB[:, :], in0=GB[:, :],
                          in1=GA[:, :], op=ALU.mult)
                    sp.on("scalar", "activation", out=GB[:, :], in_=GB[:, :],
                          func=AF.Tanh, scale=GC1)
                    sp.on("vector", "scalar_tensor_tensor", out=GB[:, :],
                          in0=GB[:, :], scalar=1.0, in1=GA[:, :], op0=ALU.add,
                          op1=ALU.mult)
                    sp.on("vector", "tensor_scalar", out=hbf[:, fc, :],
                          in0=GB[:, :], scalar1=0.5, scalar2=None, op0=ALU.mult)
                for q in range(4):
                    tt = sl * 4 + q
                    for half in range(2):
                        hc = slice(half * 512, (half + 1) * 512)
                        for fc in range(NFC):
                            sp.on("tensor", "matmul", out=psY[:, hc],
                                  lhsT=hbf[:, fc, q * 128:(q + 1) * 128],
                                  rhs=w2_s[:, fc, hc],
                                  start=(fc == 0), stop=(fc == NFC - 1))
                    sp.on("vector", "tensor_tensor", out=Y32[:, :], in0=psY[:, :],
                          in1=B2B[:, :], op=ALU.add)
                    tr = slice(tt * 128, (tt + 1) * 128)
                    if e == 0:
                        sp.on("vector", "tensor_scalar", out=OUT32[:, :],
                              in0=Y32[:, :], scalar1=SEL[:, tt, e:e + 1],
                              scalar2=None, op0=ALU.mult)
                    else:
                        sp.dma("sync", out=OUT32[:, :], in_=f0_h[tr, :])
                        sp.on("vector", "scalar_tensor_tensor", out=OUT32[:, :],
                              in0=Y32[:, :], scalar=SEL[:, tt, e:e + 1],
                              in1=OUT32[:, :], op0=ALU.mult, op1=ALU.add)
                    sp.dma("gpsimd", out=f0_h[tr, :], in_=OUT32[:, :])
        for tt in range(NT):
            r = slice(tt * 128, (tt + 1) * 128)
            sp.dma("sync", out=OUT32[:, :], in_=f0_h[r, :])
            sp.on("vector", "tensor_scalar", out=Y32[:, :], in0=OUT32[:, :],
                  scalar1=RPM[:, tt:tt + 1], scalar2=None, op0=ALU.mult)
            sp.dma("gpsimd", out=f0_h[r, :], in_=Y32[:, :])
        emit(nc, sp)
    return nc


def build_attn1():
    nc = _bass()
    dt = nc.dram_tensor
    xnf_h = dt("xnf", [ND, 128, T], BF16, kind="ExternalInput").ap()
    wqkv_h = dt("wqkvT", [D, 3 * D], F32, kind="ExternalInput").ap()
    bqkv_h = dt("bqkv", [1, 3 * D], F32, kind="ExternalInput").ap()
    wo_h = dt("woutT", [D, D], F32, kind="ExternalInput").ap()
    bo_h = dt("bout", [1, D], F32, kind="ExternalInput").ap()
    cos_h = dt("cosd", [128, NT, 64], F32, kind="ExternalInput").ap()
    sin_h = dt("sind", [128, NT, 64], F32, kind="ExternalInput").ap()
    tri_h = dt("tri", [128, 128], F32, kind="ExternalInput").ap()
    idb_h = dt("idb", [128, 128], BF16, kind="ExternalInput").ap()
    a1_h = dt("a1", [T, D], F32, kind="ExternalOutput").ap()

    sp = SP()
    es = contextlib.ExitStack()
    with es:
        xns = es.enter_context(nc.sbuf_tensor("xns", [128, ND, T], BF16))
        kf = es.enter_context(nc.sbuf_tensor("kf", [128, ND, T], BF16))
        vg = es.enter_context(nc.sbuf_tensor("vg", [128, NT, D], BF16))
        wp_s = es.enter_context(nc.sbuf_tensor("wp_s", [128, ND, D], BF16))
        wo_s = es.enter_context(nc.sbuf_tensor("wo_s", [128, ND, D], BF16))
        brow = es.enter_context(nc.sbuf_tensor("brow", [1, 3 * D], F32))
        borow = es.enter_context(nc.sbuf_tensor("borow", [1, D], F32))
        ones1 = es.enter_context(nc.sbuf_tensor("ones1", [1, 128], F32))
        idb = es.enter_context(nc.sbuf_tensor("idb_s", [128, 128], BF16))
        cosd = es.enter_context(nc.sbuf_tensor("cosd_s", [128, NT, 64], F32))
        sind = es.enter_context(nc.sbuf_tensor("sind_s", [128, NT, 64], F32))
        tri = es.enter_context(nc.sbuf_tensor("tri_s", [128, 128], F32))
        BB32 = es.enter_context(nc.sbuf_tensor("BB32", [128, D], F32))
        BO32 = es.enter_context(nc.sbuf_tensor("BO32", [128, D], F32))
        Q32 = es.enter_context(nc.sbuf_tensor("Q32", [128, D], F32))
        QS32 = es.enter_context(nc.sbuf_tensor("QS32", [128, D], F32))
        R1 = es.enter_context(nc.sbuf_tensor("R1", [128, D], F32))
        QB = es.enter_context(nc.sbuf_tensor("QB", [128, D], BF16))
        qft = es.enter_context(nc.sbuf_tensor("qft", [128, D], BF16))
        S32 = es.enter_context(nc.sbuf_tensor("S32", [128, T], F32))
        Pbf = es.enter_context(nc.sbuf_tensor("Pbf", [128, T], BF16))
        PTb = es.enter_context(nc.sbuf_tensor("PTb", [128, 128], BF16))
        st = es.enter_context(nc.sbuf_tensor("st", [128, 4], F32))
        AO32 = es.enter_context(nc.sbuf_tensor("AO32", [128, D], F32))
        AObf = es.enter_context(nc.sbuf_tensor("AObf", [128, D], BF16))
        Af = es.enter_context(nc.sbuf_tensor("Af", [128, D], BF16))
        OUT32 = es.enter_context(nc.sbuf_tensor("OUT32", [128, D], F32))
        psK = es.enter_context(nc.psum_tensor("psK", [128, D], F32))
        psT = es.enter_context(nc.psum_tensor("psT", [128, D], BF16))
        psS = es.enter_context(nc.psum_tensor("psS", [128, 512], F32))
        psPT = es.enter_context(nc.psum_tensor("psPT", [128, 128], BF16))
        psO = es.enter_context(nc.psum_tensor("psO", [128, 64], F32))
        psY = es.enter_context(nc.psum_tensor("psY", [128, D], F32))
        sp.on("vector", "memset", ap=ones1[:, :], constant=1.0)
        sp.dma("sync", out=idb[:, :], in_=idb_h)
        sp.dma("sync", out=tri[:, :], in_=tri_h)
        sp.dma("sync", out=cosd[:, :, :], in_=cos_h)
        sp.dma("sync", out=sind[:, :, :], in_=sin_h)
        sp.dma("sync", out=brow[:, :], in_=bqkv_h)
        sp.dma("sync", out=borow[:, :], in_=bo_h)
        for dc in range(ND):
            c = slice(dc * 128, (dc + 1) * 128)
            sp.dma("sync", out=xns[:, dc, :], in_=xnf_h[dc, :, :])
            sp.dma("gpsimd", out=wo_s[:, dc, :], in_=wo_h[c, :])
        for half in range(2):
            hc = slice(half * 512, (half + 1) * 512)
            sp.on("tensor", "matmul", out=psY[:, hc], lhsT=ones1[:, :],
                  rhs=borow[:, half * 512:(half + 1) * 512], start=True, stop=True)
        sp.on("vector", "tensor_copy", out=BO32[:, :], in_=psY[:, :])

        def load_w(which):
            wv = wqkv_h[:, which * D:(which + 1) * D]
            for dc in range(ND):
                sp.dma("gpsimd", out=wp_s[:, dc, :],
                       in_=wv[dc * 128:(dc + 1) * 128, :])
            boff = which * D
            for half in range(2):
                hc = slice(half * 512, (half + 1) * 512)
                sp.on("tensor", "matmul", out=psY[:, hc], lhsT=ones1[:, :],
                      rhs=brow[:, boff + half * 512:boff + (half + 1) * 512],
                      start=True, stop=True)
            sp.on("vector", "tensor_copy", out=BB32[:, :], in_=psY[:, :])

        def project(tt):
            for half in range(2):
                hc = slice(half * 512, (half + 1) * 512)
                for dc in range(ND):
                    sp.on("tensor", "matmul", out=psK[:, hc],
                          lhsT=xns[:, dc, tt * 128:(tt + 1) * 128],
                          rhs=wp_s[:, dc, hc],
                          start=(dc == 0), stop=(dc == ND - 1))
            sp.on("vector", "tensor_tensor", out=Q32[:, :], in0=psK[:, :],
                  in1=BB32[:, :], op=ALU.add)

        def rope(tt):
            qv = Q32[:, :].rearrange("p (h i two) -> p h i two", h=H, two=2)
            sv = QS32[:, :].rearrange("p (h i two) -> p h i two", h=H, two=2)
            sp.on("vector", "tensor_copy", out=sv[:, :, :, 0:1], in_=qv[:, :, :, 1:2])
            sp.on("vector", "tensor_copy", out=sv[:, :, :, 1:2], in_=qv[:, :, :, 0:1])
            Qh = Q32[:, :].rearrange("p (h j) -> p h j", h=H)
            Sh = QS32[:, :].rearrange("p (h j) -> p h j", h=H)
            Rh = R1[:, :].rearrange("p (h j) -> p h j", h=H)
            for h in range(H):
                sp.on("vector", "tensor_tensor", out=Rh[:, h, :], in0=Qh[:, h, :],
                      in1=cosd[:, tt, :], op=ALU.mult)
                sp.on("vector", "tensor_tensor", out=Sh[:, h, :], in0=Sh[:, h, :],
                      in1=sind[:, tt, :], op=ALU.mult)
                sp.on("vector", "tensor_tensor", out=Qh[:, h, :], in0=Rh[:, h, :],
                      in1=Sh[:, h, :], op=ALU.add)

        load_w(1)
        for tt in range(NT):
            project(tt)
            rope(tt)
            sp.on("vector", "tensor_copy", out=QB[:, :], in_=Q32[:, :])
            for dc in range(ND):
                c = slice(dc * 128, (dc + 1) * 128)
                sp.on("tensor", "transpose", out=psT[:, c], in_=QB[:, c],
                      identity=idb[:, :])
            sp.on("vector", "tensor_copy",
                  out=kf[:, :, tt * 128:(tt + 1) * 128],
                  in_=psT[:, :].rearrange("p (c t) -> p c t", t=128))
        load_w(2)
        for tt in range(NT):
            project(tt)
            sp.on("vector", "tensor_copy", out=vg[:, tt, :], in_=Q32[:, :])
        load_w(0)
        for qt in range(NT):
            tr = slice(qt * 128, (qt + 1) * 128)
            project(qt)
            rope(qt)
            sp.on("vector", "tensor_copy", out=QB[:, :], in_=Q32[:, :])
            for dc in range(ND):
                c = slice(dc * 128, (dc + 1) * 128)
                sp.on("tensor", "transpose", out=psT[:, c], in_=QB[:, c],
                      identity=idb[:, :])
            sp.on("vector", "tensor_copy", out=qft[:, :], in_=psT[:, :])
            qfv = qft[:, :].rearrange("p (c t) -> p c t", t=128)
            ns = qt + 1
            nk = ns * 128
            for h in range(H):
                po = (h % 2) * 64
                lq = qfv[po:po + 64, h // 2, :]
                c0 = 0
                while c0 < nk:
                    cw = min(512, nk - c0)
                    sp.on("tensor", "matmul", out=psS[:, 0:cw],
                          lhsT=lq, rhs=kf[po:po + 64, h // 2, c0:c0 + cw],
                          start=True, stop=True, skip_group_check=True)
                    sp.on("vector", "tensor_scalar", out=S32[:, c0:c0 + cw],
                          in0=psS[:, 0:cw], scalar1=0.125, scalar2=None,
                          op0=ALU.mult)
                    c0 += cw
                sp.on("vector", "tensor_tensor",
                      out=S32[:, qt * 128:(qt + 1) * 128],
                      in0=S32[:, qt * 128:(qt + 1) * 128], in1=tri[:, :],
                      op=ALU.add)
                sp.on("vector", "reduce_max", out=st[:, 0:1], in_=S32[:, 0:nk],
                      axis=AX.X, negate=True)
                sp.on("scalar", "activation", out=Pbf[:, 0:nk], in_=S32[:, 0:nk],
                      func=AF.Exp, bias=st[:, 0:1], accum_out=st[:, 1:2])
                sp.on("vector", "reciprocal", out=st[:, 2:3], in_=st[:, 1:2])
                for stt in range(ns):
                    sp.on("tensor", "transpose", out=psPT[:, :],
                          in_=Pbf[:, stt * 128:(stt + 1) * 128],
                          identity=idb[:, :])
                    sp.on("vector", "tensor_copy", out=PTb[:, :], in_=psPT[:, :])
                    sp.on("tensor", "matmul", out=psO[:, :], lhsT=PTb[:, :],
                          rhs=vg[:, stt, h * 64:(h + 1) * 64],
                          start=(stt == 0), stop=(stt == ns - 1),
                          skip_group_check=True)
                sp.on("vector", "tensor_scalar", out=AO32[:, h * 64:(h + 1) * 64],
                      in0=psO[:, :], scalar1=st[:, 2:3], scalar2=None,
                      op0=ALU.mult)
            sp.on("vector", "tensor_copy", out=AObf[:, :], in_=AO32[:, :])
            for dc in range(ND):
                c = slice(dc * 128, (dc + 1) * 128)
                sp.on("tensor", "transpose", out=psT[:, c], in_=AObf[:, c],
                      identity=idb[:, :])
            sp.on("vector", "tensor_copy", out=Af[:, :], in_=psT[:, :])
            for half in range(2):
                hc = slice(half * 512, (half + 1) * 512)
                for dc in range(ND):
                    sp.on("tensor", "matmul", out=psY[:, hc],
                          lhsT=Af[:, dc * 128:(dc + 1) * 128],
                          rhs=wo_s[:, dc, hc],
                          start=(dc == 0), stop=(dc == ND - 1))
            sp.on("vector", "tensor_tensor", out=OUT32[:, :], in0=psY[:, :],
                  in1=BO32[:, :], op=ALU.add)
            sp.dma("gpsimd", out=a1_h[tr, :], in_=OUT32[:, :])
        emit(nc, sp)
    return nc


def build_ffn1():
    nc = _bass()
    dt = nc.dram_tensor
    xnf_h = dt("xnf", [ND, 128, T], BF16, kind="ExternalInput").ap()
    gl_h = dt("glog", [T, E], F32, kind="ExternalInput").ap()
    ew_h = dt("ew", [E, D, D], F32, kind="ExternalInput").ap()
    eb_h = dt("eb", [E, D], F32, kind="ExternalInput").ap()
    x1_h = dt("x1", [T, D], F32, kind="ExternalInput").ap()
    a1_h = dt("a1", [T, D], F32, kind="ExternalInput").ap()
    idx_h = dt("idx8", [128, E], F32, kind="ExternalInput").ap()
    y_h = dt("y", [T, D], F32, kind="ExternalOutput").ap()

    sp = SP()
    es = contextlib.ExitStack()
    with es:
        xns = es.enter_context(nc.sbuf_tensor("xns", [128, ND, T], BF16))
        facc = es.enter_context(nc.sbuf_tensor("facc", [128, NT, D], F32))
        ew_s = es.enter_context(nc.sbuf_tensor("ew_s", [128, ND, D], BF16))
        ebrow = es.enter_context(nc.sbuf_tensor("ebrow", [1, D], F32))
        EB = es.enter_context(nc.sbuf_tensor("EB", [128, D], F32))
        ones1 = es.enter_context(nc.sbuf_tensor("ones1", [1, 128], F32))
        idx8 = es.enter_context(nc.sbuf_tensor("idx8_s", [128, E], F32))
        l8 = es.enter_context(nc.sbuf_tensor("l8", [128, E], F32))
        p8 = es.enter_context(nc.sbuf_tensor("p8", [128, E], F32))
        r8a = es.enter_context(nc.sbuf_tensor("r8a", [128, E], F32))
        r8b = es.enter_context(nc.sbuf_tensor("r8b", [128, E], F32))
        r8c = es.enter_context(nc.sbuf_tensor("r8c", [128, E], F32))
        COMB = es.enter_context(nc.sbuf_tensor("COMB", [128, NT, E], F32))
        st = es.enter_context(nc.sbuf_tensor("st", [128, 6], F32))
        Y32 = es.enter_context(nc.sbuf_tensor("Y32", [128, D], F32))
        XA = es.enter_context(nc.sbuf_tensor("XA", [128, D], F32))
        XB = es.enter_context(nc.sbuf_tensor("XB", [128, D], F32))
        psY = es.enter_context(nc.psum_tensor("psY", [128, D], F32))
        sp.on("vector", "memset", ap=ones1[:, :], constant=1.0)
        sp.on("vector", "memset",
              ap=facc[:, :, :].rearrange("p a b -> p (a b)"), constant=0.0)
        sp.dma("sync", out=idx8[:, :], in_=idx_h)
        for dc in range(ND):
            sp.dma("sync", out=xns[:, dc, :], in_=xnf_h[dc, :, :])
        for tt in range(NT):
            r = slice(tt * 128, (tt + 1) * 128)
            sp.dma("sync", out=l8[:, :], in_=gl_h[r, :])
            sp.on("vector", "tensor_scalar", out=l8[:, :], in0=l8[:, :],
                  scalar1=G1EPS, scalar2=None, op0=ALU.max)
            sp.on("vector", "reduce_max", out=st[:, 0:1], in_=l8[:, :],
                  axis=AX.X, negate=True)
            sp.on("scalar", "activation", out=p8[:, :], in_=l8[:, :],
                  func=AF.Exp, bias=st[:, 0:1], accum_out=st[:, 1:2])
            sp.on("vector", "reciprocal", out=st[:, 2:3], in_=st[:, 1:2])
            sp.on("vector", "tensor_scalar", out=p8[:, :], in0=p8[:, :],
                  scalar1=st[:, 2:3], scalar2=None, op0=ALU.mult)
            sp.on("vector", "reduce_max", out=st[:, 3:4], in_=p8[:, :],
                  axis=AX.X, negate=True)
            sp.on("vector", "tensor_scalar", out=r8a[:, :], in0=p8[:, :],
                  scalar1=st[:, 3:4], scalar2=0.0, op0=ALU.add, op1=ALU.is_equal)
            sp.on("vector", "tensor_tensor", out=r8b[:, :], in0=r8a[:, :],
                  in1=idx8[:, :], op=ALU.mult)
            sp.on("vector", "scalar_tensor_tensor", out=r8b[:, :], in0=r8a[:, :],
                  scalar=-8.0, in1=r8b[:, :], op0=ALU.mult, op1=ALU.add)
            sp.on("vector", "tensor_scalar", out=r8b[:, :], in0=r8b[:, :],
                  scalar1=8.0, scalar2=None, op0=ALU.add)
            sp.on("vector", "tensor_reduce", out=st[:, 4:5], in_=r8b[:, :],
                  axis=AX.X, op=ALU.min)
            sp.on("vector", "tensor_scalar", out=r8a[:, :], in0=idx8[:, :],
                  scalar1=st[:, 4:5], scalar2=None, op0=ALU.is_equal)
            sp.on("vector", "scalar_tensor_tensor", out=r8b[:, :], in0=r8a[:, :],
                  scalar=-1e30, in1=p8[:, :], op0=ALU.mult, op1=ALU.add)
            sp.on("vector", "reduce_max", out=st[:, 3:4], in_=r8b[:, :],
                  axis=AX.X, negate=True)
            sp.on("vector", "tensor_scalar", out=r8c[:, :], in0=r8b[:, :],
                  scalar1=st[:, 3:4], scalar2=0.0, op0=ALU.add, op1=ALU.is_equal)
            sp.on("vector", "tensor_tensor", out=r8b[:, :], in0=r8c[:, :],
                  in1=idx8[:, :], op=ALU.mult)
            sp.on("vector", "scalar_tensor_tensor", out=r8b[:, :], in0=r8c[:, :],
                  scalar=-8.0, in1=r8b[:, :], op0=ALU.mult, op1=ALU.add)
            sp.on("vector", "tensor_scalar", out=r8b[:, :], in0=r8b[:, :],
                  scalar1=8.0, scalar2=None, op0=ALU.add)
            sp.on("vector", "tensor_reduce", out=st[:, 4:5], in_=r8b[:, :],
                  axis=AX.X, op=ALU.min)
            sp.on("vector", "tensor_scalar", out=r8c[:, :], in0=idx8[:, :],
                  scalar1=st[:, 4:5], scalar2=None, op0=ALU.is_equal)
            sp.on("vector", "tensor_tensor", out=r8a[:, :], in0=r8a[:, :],
                  in1=r8c[:, :], op=ALU.add)
            sp.on("vector", "tensor_scalar", out=r8b[:, :], in0=p8[:, :],
                  scalar1=WTHR, scalar2=None, op0=ALU.max)
            sp.on("vector", "tensor_tensor", out=r8b[:, :], in0=r8b[:, :],
                  in1=r8a[:, :], op=ALU.mult)
            sp.on("vector", "reduce_sum", out=st[:, 5:6], in_=r8b[:, :], axis=AX.X)
            sp.on("vector", "reciprocal", out=st[:, 3:4], in_=st[:, 5:6])
            sp.on("vector", "tensor_scalar", out=COMB[:, tt, :], in0=r8b[:, :],
                  scalar1=st[:, 3:4], scalar2=0.5, op0=ALU.mult, op1=ALU.mult)
        for e in range(E):
            ewv = ew_h[e].rearrange("(c p) f -> p c f", p=128)
            for dc in range(ND):
                sp.dma("gpsimd", out=ew_s[:, dc, :], in_=ewv[:, dc, :])
            sp.dma("sync", out=ebrow[:, :], in_=eb_h[e:e + 1, :])
            for half in range(2):
                hc = slice(half * 512, (half + 1) * 512)
                sp.on("tensor", "matmul", out=psY[:, hc], lhsT=ones1[:, :],
                      rhs=ebrow[:, half * 512:(half + 1) * 512], start=True,
                      stop=True)
            sp.on("vector", "tensor_copy", out=EB[:, :], in_=psY[:, :])
            for tt in range(NT):
                for half in range(2):
                    hc = slice(half * 512, (half + 1) * 512)
                    for dc in range(ND):
                        sp.on("tensor", "matmul", out=psY[:, hc],
                              lhsT=xns[:, dc, tt * 128:(tt + 1) * 128],
                              rhs=ew_s[:, dc, hc],
                              start=(dc == 0), stop=(dc == ND - 1))
                sp.on("vector", "tensor_tensor", out=Y32[:, :], in0=psY[:, :],
                      in1=EB[:, :], op=ALU.add)
                sp.on("vector", "scalar_tensor_tensor", out=facc[:, tt, :],
                      in0=Y32[:, :], scalar=COMB[:, tt, e:e + 1],
                      in1=facc[:, tt, :], op0=ALU.mult, op1=ALU.add)
        for tt in range(NT):
            r = slice(tt * 128, (tt + 1) * 128)
            sp.dma("sync", out=XA[:, :], in_=x1_h[r, :])
            sp.dma("sync", out=XB[:, :], in_=a1_h[r, :])
            sp.on("vector", "tensor_tensor", out=XA[:, :], in0=XA[:, :],
                  in1=XB[:, :], op=ALU.add)
            sp.on("vector", "tensor_tensor", out=XA[:, :], in0=XA[:, :],
                  in1=facc[:, tt, :], op=ALU.add)
            sp.dma("gpsimd", out=y_h[r, :], in_=XA[:, :])
        emit(nc, sp)
    return nc


_cache = {}


def _get(name, fn):
    if name not in _cache:
        _cache[name] = fn()
    return _cache[name]


def _consts():
    idb = np.eye(128, dtype=BF)
    tri = np.where(np.arange(128)[None, :] <= np.arange(128)[:, None],
                   0.0, -1e30).astype(np.float32)
    inv = (10000.0 ** (-np.arange(0, DH, 2, dtype=np.float64) / DH))
    t = np.arange(T, dtype=np.float64)
    th = t[:, None] * inv[None, :]
    cosd = np.repeat(np.cos(th), 2, axis=1)
    sin2 = np.sin(th)
    sind = np.empty((T, DH))
    sind[:, 0::2] = -sin2
    sind[:, 1::2] = sin2
    cosd = cosd.reshape(NT, 128, DH).transpose(1, 0, 2).astype(np.float32).copy()
    sind = sind.reshape(NT, 128, DH).transpose(1, 0, 2).astype(np.float32).copy()
    idx8 = np.broadcast_to(np.arange(E, dtype=np.float32), (128, E)).copy()
    return idb, tri, cosd, sind, idx8


def _run(nc, maps):
    res = run_bass_kernel_spmd(nc, maps, core_ids=list(range(8)))
    return res.results


def kernel(**ins):
    idb, tri, cosd, sind, idx8 = _consts()
    cc = np.ascontiguousarray
    x = cc(np.asarray(ins["L0_attn_x"], dtype=np.float32))

    nc = _get("ln0", lambda: build_ln_gate(False))
    m = dict(gw=cc(ins["L0_ffn_switch_w"]),
             gb=cc(ins["L0_ffn_switch_b"]).reshape(1, E), idb=idb)
    r1 = _run(nc, [dict(m, x=x[c]) for c in range(8)])

    nc = _get("attn0", build_attn0)
    m = dict(wqT=cc(ins["L0_attn_Wq"].T), wkT=cc(ins["L0_attn_Wk"].T),
             wvT=cc(ins["L0_attn_Wv"].T), woT=cc(ins["L0_attn_Wo"].T),
             bq=cc(ins["L0_attn_bq"]).reshape(1, D),
             bk=cc(ins["L0_attn_bk"]).reshape(1, D),
             bv=cc(ins["L0_attn_bv"]).reshape(1, D),
             bo=cc(ins["L0_attn_bo"]).reshape(1, D), idb=idb)
    r2 = _run(nc, [dict(m, xnf=r1[c]["xnf"]) for c in range(8)])

    nc = _get("ffn0", build_ffn0)
    m = dict(w1=cc(ins["L0_ffn_w1"]), b1=cc(ins["L0_ffn_b1"]),
             w2=cc(ins["L0_ffn_w2"]), b2=cc(ins["L0_ffn_b2"]))
    r3 = _run(nc, [dict(m, xnf=r1[c]["xnf"], glog=r1[c]["glog"])
                   for c in range(8)])

    nc = _get("ln1", lambda: build_ln_gate(True))
    m = dict(gw=cc(ins["L1_ffn_gate_W"]),
             gb=cc(ins["L1_ffn_gate_b"]).reshape(1, E), idb=idb)
    r4 = _run(nc, [dict(m, x=x[c], a0=r2[c]["a0"], f0=r3[c]["f0"])
                   for c in range(8)])

    nc = _get("attn1", build_attn1)
    m = dict(wqkvT=cc(ins["L1_attn_w_qkv"].T),
             bqkv=cc(ins["L1_attn_b_qkv"]).reshape(1, 3 * D),
             woutT=cc(ins["L1_attn_w_out"].T),
             bout=cc(ins["L1_attn_b_out"]).reshape(1, D),
             cosd=cosd, sind=sind, tri=tri, idb=idb)
    r5 = _run(nc, [dict(m, xnf=r4[c]["xnf"]) for c in range(8)])

    nc = _get("ffn1", build_ffn1)
    m = dict(ew=cc(ins["L1_ffn_expert_W"]), eb=cc(ins["L1_ffn_expert_b"]),
             idx8=idx8)
    r6 = _run(nc, [dict(m, xnf=r4[c]["xnf"], glog=r4[c]["glog"],
                        x1=r4[c]["x1"], a1=r5[c]["a1"]) for c in range(8)])

    return np.stack([r6[c]["y"] for c in range(8)], axis=0).astype(np.float32)
